# version 05 (final)
# baseline (speedup 1.0000x reference)
import sys

sys.path.insert(0, "/opt/trn_rl_repo")
from contextlib import ExitStack

import numpy as np

import concourse.bass as bass
import concourse.mybir as mybir
from concourse.bass_utils import run_bass_kernel_spmd

F32 = mybir.dt.float32
BF16 = mybir.dt.bfloat16
I32 = mybir.dt.int32
AF = mybir.ActivationFunctionType
ALU = mybir.AluOpType
AX = mybir.AxisListType

NCORES = 8
T = 2048
D = 2048
DI = 4096
NH = 64
PH = 64
NS = 64
L = 128
NCH = T // L
SC = 512
NSC = T // SC
CPS = SC // L
DPROJ = 8384
E = 8
HF = 1408
HT = HF // 128
KD = D // 128
KDI = DI // 128
SSD_SCALE = 0.125
EPS_LN = 1e-5
EPS_RMS = 1e-6
INV_2PI = float(1.0 / (2.0 * np.pi))
TWO_PI = float(2.0 * np.pi)


class Sem:
    def __init__(self, st, nc, name, chained=False):
        self.h = st.enter_context(nc.semaphore(name))
        self.n = 0
        self.chained = chained

    def plan(self, amount):
        self.n += amount
        return self.n

    def at(self):
        return (self, self.n)


class Prog:
    def __init__(self):
        self.steps = []

    def do(self, eng, fn, wait=(), inc=None, inc_amount=1):
        waits = [(s.h, v) for (s, v) in wait if v > 0]
        entry_inc = None
        if inc is not None:
            if inc.chained and inc.n > 0:
                waits.append((inc.h, inc.n))
            inc.plan(inc_amount)
            entry_inc = (inc.h, inc_amount)
        self.steps.append((eng, waits, fn, entry_inc))
        return inc.n if inc is not None else None

    def dma(self, eng, fn, wait=(), inc=None):
        return self.do(eng, fn, wait=wait, inc=inc, inc_amount=16)

    def replay(self, block):
        def runner(name):
            def _(engobj):
                for eng, waits, fn, inc in self.steps:
                    if eng != name:
                        continue
                    for (h, v) in waits:
                        engobj.wait_ge(h, v)
                    inst = fn(engobj)
                    if inc is not None:
                        inst.then_inc(inc[0], inc[1])
            return _

        block.tensor(runner("tensor"))
        block.vector(runner("vector"))
        block.scalar(runner("scalar"))
        block.gpsimd(runner("gpsimd"))
        block.sync(runner("sync"))


def build(stage=99, debug=False):
    nc = bass.Bass("TRN2", target_bir_lowering=False, debug=False)

    x_in = nc.dram_tensor("x_in", [T, D], F32, kind="ExternalInput")
    w_dtbc = nc.dram_tensor("w_dtbc", [D, 192], F32, kind="ExternalInput")
    dt_bias = nc.dram_tensor("dt_bias", [NH, 1], F32, kind="ExternalInput")
    a_log = nc.dram_tensor("a_log", [NH, 1], F32, kind="ExternalInput")
    d_param = nc.dram_tensor("d_param", [1, NH], F32, kind="ExternalInput")
    b_bias = nc.dram_tensor("b_bias", [1, NH * NS], F32, kind="ExternalInput")
    c_bias = nc.dram_tensor("c_bias", [1, NH * NS], F32, kind="ExternalInput")
    freqs_in = nc.dram_tensor("freqs_in", [1, NH * 32], F32, kind="ExternalInput")
    norm_in = nc.dram_tensor("norm_in", [1, DI], F32, kind="ExternalInput")
    wg_in = nc.dram_tensor("wg_in", [E, D], F32, kind="ExternalInput")
    wzxs_share = nc.dram_tensor("wzxs_share", [D // 8, 2 * DI], F32, kind="ExternalInput")
    wout_share = nc.dram_tensor("wout_share", [DI // 8, D], F32, kind="ExternalInput")
    w1_mine = nc.dram_tensor("w1_mine", [HF, D], F32, kind="ExternalInput")
    w2_mine = nc.dram_tensor("w2_mine", [HF, D], F32, kind="ExternalInput")
    w3_mine = nc.dram_tensor("w3_mine", [D, HF], F32, kind="ExternalInput")
    mask_l = nc.dram_tensor("mask_l", [1, 1], F32, kind="ExternalInput")
    out_t = nc.dram_tensor("out", [T, D], F32, kind="ExternalOutput")

    cc_zxs_i = nc.dram_tensor("cc_zxs_i", [D // 8, 2 * DI], BF16)
    cc_zxs_o = nc.dram_tensor("cc_zxs_o", [D, 2 * DI], BF16)
    cc_wo_i = nc.dram_tensor("cc_wo_i", [DI // 8, D], BF16)
    cc_wo_o = nc.dram_tensor("cc_wo_o", [DI, D], BF16)
    cc_w1_i = nc.dram_tensor("cc_w1_i", [HF, D], BF16)
    cc_w1_o = nc.dram_tensor("cc_w1_o", [E * HF, D], BF16)
    cc_w2_i = nc.dram_tensor("cc_w2_i", [HF, D], BF16)
    cc_w2_o = nc.dram_tensor("cc_w2_o", [E * HF, D], BF16)
    cc_w3_i = nc.dram_tensor("cc_w3_i", [D, HF], BF16)
    cc_w3_o = nc.dram_tensor("cc_w3_o", [E * D, HF], BF16)
    cc_dts_i = nc.dram_tensor("cc_dts_i", [NH, 1], F32)
    cc_dts_o = nc.dram_tensor("cc_dts_o", [2 * NH, 1], F32)
    cc_h_i = nc.dram_tensor("cc_h_i", [128, 32 * PH], F32)
    cc_h_o = nc.dram_tensor("cc_h_o", [256, 32 * PH], F32)

    xs_sp = nc.dram_tensor("xs_sp", [T, DI], BF16)
    z_sp = nc.dram_tensor("z_sp", [T, DI], BF16)
    bt_sp = nc.dram_tensor("bt_sp", [NCH, 32, 128, 128], BF16)
    ct_sp = nc.dram_tensor("ct_sp", [NCH, 32, 128, 128], BF16)
    hloc_sp = nc.dram_tensor("hloc_sp", [NCH, 128, 32 * PH], BF16)
    x2_sp = nc.dram_tensor("x2_sp", [T, D], BF16)
    xln2_sp = nc.dram_tensor("xln2_sp", [D, T], BF16)
    y_hbm = nc.dram_tensor("y_hbm", [D, T], F32)
    trig_sp = nc.dram_tensor("trig_sp", [NCH, 128, 2 * NH * 32], BF16)
    ynt_sp = nc.dram_tensor("ynt_sp", [NCH, KDI, 128, 128], BF16)
    clch_sp = nc.dram_tensor("clch_sp", [2, NH, T], F32)
    wg_sp = nc.dram_tensor("wg_sp", [E, T], F32)

    dbg = {}

    def mkdbg(name, shape, dt=F32):
        if debug:
            dbg[name] = nc.dram_tensor(name, shape, dt, kind="ExternalOutput")
        return dbg.get(name)

    st = ExitStack()
    A = lambda name, shape, dt: st.enter_context(nc.sbuf_tensor(name, shape, dt))

    dtcum_sb = A("dtcum_sb", [128, T], F32)
    dt_sb = dtcum_sb[0:64, :]
    cum_sb = dtcum_sb[64:128, :]
    cuml_full = A("cuml_full", [NH + 2, T], F32)
    cuml_sb = cuml_full[0:NH, :]
    bcr_sb = A("bcr_sb", [128, T], F32)
    braw_sb = bcr_sb[0:64, :]
    craw_sb = bcr_sb[64:128, :]
    chend_sb = A("chend_sb", [NH, NCH + 1], F32)
    dall_sb = A("dall_sb", [NH, NCH], F32)
    etot_sb = A("etot_sb", [NH, NCH], F32)
    m1_sb = A("m1_sb", [128, NCH], F32)
    rs1_sb = A("rs1_sb", [128, NCH], F32)
    m2_sb = A("m2_sb", [128, NCH], F32)
    rs2_sb = A("rs2_sb", [128, NCH], F32)
    dtb_sb = A("dtb_sb", [NH, 1], F32)
    nega_sb = A("nega_sb", [NH, 1], F32)
    dts_off = A("dts_off", [NH, 1], F32)
    cl0_sb = A("cl0_sb", [NH, 1], F32)
    rsb_sb = cuml_full[NH:NH + 1, :]
    rsc_sb = cuml_full[NH + 1:NH + 2, :]
    sums_sb = A("sums_sb", [128, 18], F32)
    dp_sb = A("dp_sb", [1, NH], F32)
    mask_sb = A("mask_sb", [1, 1], F32)
    tp_sb = A("tp_sb", [128, 456], F32)
    etmp_sb = A("etmp_sb", [NH, L], F32)
    eye_sb = A("eye_sb", [128, 128], BF16)
    eyef_sb = A("eyef_sb", [128, 128], F32)
    tri_sb = A("tri_sb", [128, 128], BF16)
    ones1f = A("ones1f", [1, 128], F32)
    onesb_sb = A("onesb_sb", [1, 128], BF16)
    eps1_sb = A("eps1_sb", [128, 1], F32)
    eps2_sb = A("eps2_sb", [128, 1], F32)
    x_f = A("x_f", [128, D], F32)
    xln_tok = A("xln_tok", [128, D], BF16)
    xln_fm = A("xln_fm", [128, SC * KD], BF16)
    wk_bf = A("wk_bf", [128, 16 * 512], BF16)
    ch_a = A("ch_a", [128, DI], BF16)
    ch_b = A("ch_b", [128, DI], BF16)
    ch_c = A("ch_c", [128, DI], BF16)
    ch_d = A("ch_d", [128, DI], BF16)
    ch_e = A("ch_e", [128, DI], F32)
    ch_f = A("ch_f", [128, 2048], F32)
    phii_sb = A("phii_sb", [128, 1024], I32)
    stg_sb = A("stg_sb", [128, 2048], BF16)
    stgf_sb = A("stgf_sb", [128, 1024], F32)
    bt_sb = A("bt_sb", [128, 1024], BF16)
    ct_sb = A("ct_sb", [128, 1024], BF16)
    hin_sb = A("hin_sb", [128, 32 * PH], BF16)
    ar_bf = A("ar_bf", [128, 19456], BF16)
    ar_f = A("ar_f", [128, 4096], F32)

    hrun_sb = ar_f[:, 2048:4096]
    dfl_sb = ar_f[0:2, 0:1024]
    dfr_sb = ar_f[0:2, 1024:2048]
    wgtt_sb = ar_f[0:8, 0:2048]
    stgf_sb = ar_f[:, 2048:3072]
    wdtbc_sb = ar_bf[:, 16384:19456]
    biasb_v = ar_bf[:, 0:4096]
    biasc_v = ar_bf[:, 4096:8192]
    trig_v = ar_bf[:, 8192:12288]
    chtmp_v = ar_bf[:, 12288:16384]
    freq2_v = ar_f[:, 0:2048]
    nrmw_v = ar_bf[:, 0:4096]
    ynt_v = ar_bf[:, 4096:8192]
    gm_v = ar_bf[:, 8192:10240]
    drep_v = ar_bf[:, 10240:14336]
    hk_v = ar_bf[:, 14336:16384]
    hlr_v = ar_bf[:, 16384:18432]
    xln2_v = wk_bf[:, 0:8192]
    h_v = ar_bf[:, 8192:13824]
    w12_v = ar_bf[:, 13824:16640]
    w3t_v = ar_bf[:, 16640:18688]
    scs_sb = ar_f[:, 3072:3200]
    wgt_sb = ar_f[:, 3200:3328]
    p12_sb = ar_f[:, 3328:3392]

    ps = [st.enter_context(nc.psum_tensor(f"ps{i}", [128, 512], F32)) for i in range(6)]
    pt = [st.enter_context(nc.psum_tensor(f"pt{i}", [128, 512], BF16)) for i in range(2)]

    s_dw = Sem(st, nc, "s_dw", chained=True)
    s_dg = Sem(st, nc, "s_dg", chained=True)
    s_cc = Sem(st, nc, "s_cc", chained=True)
    s_te = Sem(st, nc, "s_te")
    s_ve = Sem(st, nc, "s_ve", chained=True)
    s_sc = Sem(st, nc, "s_sc", chained=True)
    s_out = Sem(st, nc, "s_out", chained=True)
    s_gp = Sem(st, nc, "s_gp", chained=True)

    pr = Prog()
    blk = st.enter_context(nc.Block())
    psf = {i: [] for i in range(6)}

    def dsync(dst, src, wait=()):
        return pr.dma("sync", lambda e, d=dst, s=src: e.dma_start(out=d, in_=s),
                      wait=wait, inc=s_dw)

    def dgps(dst, src, wait=(), accum=None):
        if accum is None:
            return pr.dma("gpsimd", lambda e, d=dst, s=src: e.dma_start(out=d, in_=s),
                          wait=wait, inc=s_dg)
        return pr.dma("gpsimd",
                      lambda e, d=dst, s=src, a=accum: e.dma_start(out=d, in_=s, accum_op=a),
                      wait=wait, inc=s_dg)

    def dout(dst, src, wait=()):
        return pr.dma("sync", lambda e, d=dst, s=src: e.dma_start(out=d, in_=s),
                      wait=wait, inc=s_out)

    dsync(dtb_sb[:], dt_bias[:])
    dsync(nega_sb[:], a_log[:])
    dsync(mask_sb[:], mask_l[:])
    w_small = s_dw.n
    pr.do("scalar", lambda e: e.activation(out=nega_sb[:], in_=nega_sb[:], func=AF.Exp),
          wait=[(s_dw, w_small)], inc=s_sc)
    pr.do("vector", lambda e: e.tensor_scalar_mul(nega_sb[:], nega_sb[:], -1.0),
          wait=[s_sc.at()], inc=s_ve)
    v_nega = s_ve.n

    for t_ in (eye_sb, eyef_sb, tri_sb):
        pr.do("vector", lambda e, t=t_: e.memset(t[:], 1.0), inc=s_ve)
    pr.do("vector", lambda e: e.memset(ones1f[:], 1.0), inc=s_ve)
    pr.do("vector", lambda e: e.memset(onesb_sb[:], 1.0), inc=s_ve)
    pr.do("vector", lambda e: e.memset(eps1_sb[:], EPS_LN), inc=s_ve)
    pr.do("vector", lambda e: e.memset(eps2_sb[:], EPS_RMS), inc=s_ve)
    v_csts = s_ve.n
    for t_, op in ((eye_sb, ALU.is_equal), (eyef_sb, ALU.is_equal), (tri_sb, ALU.is_ge)):
        pr.do("gpsimd", lambda e, t=t_, o=op: e.affine_select(
            out=t[:], in_=t[:], pattern=[[1, 128]], compare_op=o, fill=0.0,
            channel_multiplier=-1), wait=[(s_ve, v_csts)], inc=s_gp)
    g_consts = s_gp.n

    dsync(freq2_v[0:1, :], freqs_in[:])
    w_crows = s_dw.n
    dgps(biasb_v[0:1, 0:4096], b_bias[:])
    dgps(biasc_v[0:1, 0:4096], c_bias[:])
    g_crows = s_dg.n
    pr.do("vector", lambda e: e.tensor_scalar_mul(freq2_v[0:1, :], freq2_v[0:1, :], INV_2PI),
          wait=[(s_dw, w_crows)], inc=s_ve)
    v_f2row = s_ve.n

    def bcast_row(dst, src_row_ap, nparts, wait, f32=True, permute=False):
        ones = ones1f if src_row_ap.dtype == F32 else onesb_sb
        if permute:
            full = src_row_ap[0:1, :].rearrange("o (h n) -> o h n", n=NS)
            for b in range(8):
                h0 = b * 8
                for half in range(2):
                    src = full[:, h0:h0 + 8, half::2]
                    psum = ps[5]
                    pr.do("tensor", lambda e, s=src, p=psum, n=nparts:
                          e.matmul(p[0:n, 0:256], ones[0:1, 0:n], s,
                                   start=True, stop=True),
                          wait=list(wait) + [s_sc.at()], inc=s_te)
                    wt = s_te.n
                    out = dst[0:nparts, :].rearrange(
                        "p (h x f) -> p h x f", h=NH, x=2, f=32)[:, h0:h0 + 8,
                                                                half:half + 1, :]
                    pr.do("scalar", lambda e, p=psum, o=out, n=nparts:
                          e.copy(out=o, in_=p[0:n, 0:256].rearrange(
                              "p (h o f) -> p h o f", h=8, o=1, f=32)),
                          wait=[(s_te, wt)], inc=s_sc)
            return
        W = src_row_ap.shape[-1]
        nblocks = (W + 511) // 512
        for b in range(nblocks):
            w0, w1 = b * 512, min(W, (b + 1) * 512)
            src = src_row_ap[0:1, w0:w1]
            psum = ps[5]
            pr.do("tensor", lambda e, s=src, p=psum, n=nparts, w=w1 - w0:
                  e.matmul(p[0:n, 0:w], ones[0:1, 0:n], s, start=True, stop=True),
                  wait=list(wait) + [s_sc.at()], inc=s_te)
            wt = s_te.n
            pr.do("scalar", lambda e, p=psum, d=dst, a=w0, w=w1 - w0, n=nparts:
                  e.copy(out=d[0:n, a:a + w], in_=p[0:n, 0:w]),
                  wait=[(s_te, wt)], inc=s_sc)

    pr.do("tensor", lambda e: e.matmul(ps[4][0:128, 0:1], ones1f[0:1, 0:128],
          mask_sb[0:1, 0:1], start=True, stop=True),
          wait=[(s_dw, w_small), (s_ve, v_csts)], inc=s_te)
    te_mask = s_te.n
    pr.do("scalar", lambda e: e.copy(out=sums_sb[:, 6:7], in_=ps[4][0:128, 0:1]),
          wait=[(s_te, te_mask)], inc=s_sc)
    sc_mask = s_sc.n
    maskrep = sums_sb[:, 6:7]

    bcast_row(freq2_v, freq2_v, 128, [(s_ve, v_f2row)])
    bcast_row(biasb_v, biasb_v, 128, [(s_dg, g_crows), (s_ve, v_csts)], permute=True)
    bcast_row(biasc_v, biasc_v, 128, [(s_dg, g_crows), (s_ve, v_csts)], permute=True)
    sc_bcasts = s_sc.n

    def convert_rows(src, dst, rows, cols):
        n = rows // 128
        stores = [0, 0]
        for i in range(n):
            half = min(cols, 4096)
            for c0 in range(0, cols, half):
                buf = (wk_bf[:, 0:half] if (i % 2 == 0) else xln_fm[:, 0:half])
                ld = dgps(buf, src[i * 128:(i + 1) * 128, c0:c0 + half],
                          wait=[(s_dg, stores[i % 2])])
                stores[i % 2] = dgps(dst[i * 128:(i + 1) * 128, c0:c0 + half], buf,
                                     wait=[(s_dg, ld)])

    pt_evac = {0: 0, 1: 0}
    convT_te_last = [0, 0]
    convT_st_last = [0, 0]

    def convert_w_transposed(src, dst, rows, cols):
        n = rows // 128
        nt = cols // 128
        ng = (nt + 3) // 4
        for i in range(n):
            par = i % 2
            base = wk_bf if par == 0 else xln_fm
            tbuf = base[:, 0:cols]
            sbuf = base[:, 4096:4096 + cols]
            ld = dgps(tbuf, src[i * 128:(i + 1) * 128, :],
                      wait=[(s_te, convT_te_last[par])])
            for g in range(ng):
                b = g % 2
                j0, j1 = g * 4, min(nt, g * 4 + 4)
                for j in range(j0, j1):
                    w = ([(s_dg, ld), (s_sc, pt_evac[b]), (s_gp, g_consts)]
                         if j == j0 else [])
                    pr.do("tensor", lambda e, j=j, b=b, t=tbuf: e.transpose(
                        pt[b][:, (j % 4) * 128:(j % 4 + 1) * 128],
                        t[:, j * 128:(j + 1) * 128], eye_sb[:]),
                        wait=w, inc=s_te)
                wt = s_te.n
                pr.do("scalar", lambda e, b=b, j0=j0, j1=j1, sb=sbuf: e.copy(
                    out=sb[:, j0 * 128:j1 * 128],
                    in_=pt[b][:, 0:(j1 - j0) * 128]),
                    wait=[(s_te, wt), (s_out, convT_st_last[par])], inc=s_sc)
                pt_evac[b] = s_sc.n
            convT_te_last[par] = s_te.n
            convT_st_last[par] = dout(dst[i * 128:(i + 1) * 128, :], sbuf,
                                      wait=[s_sc.at()])

    if stage >= -2:
        convert_rows(wzxs_share, cc_zxs_i, 256, 2 * DI)
    if stage >= 3:
        convert_rows(wout_share, cc_wo_i, 512, D)
        convert_w_transposed(w1_mine, cc_w1_i, HF, D)
        convert_w_transposed(w2_mine, cc_w2_i, HF, D)
        convert_w_transposed(w3_mine, cc_w3_i, D, HF)
    g_zxs = s_dg.n
    conv_done = g_zxs
    convT_te = s_te.n
    convT_out = s_out.n
    if stage >= -1:
        pr.do("gpsimd", lambda e: e.collective_compute(
            "AllGather", ALU.bypass, replica_groups=[list(range(NCORES))],
            ins=[cc_zxs_i.ap().opt()], outs=[cc_zxs_o.ap().opt()]),
            wait=[(s_dg, g_zxs)], inc=s_cc)
    cc_zxs = s_cc.n

    for k in range(KD if stage >= -2 else 0):
        dgps(wdtbc_sb[:, k * 192:(k + 1) * 192], w_dtbc[k * 128:(k + 1) * 128, :])
    g_dtbc = s_dg.n

    def emit_xln_superchunk(sc, m_st, rs_st, x_src, compute_stats, wait_x=(),
                            cast=False, fm_free=()):
        for cc in range(CPS):
            ch = sc * CPS + cc
            ld = dgps if cast else dsync
            wx = ld(x_f[:], x_src[ch * 128:(ch + 1) * 128, :],
                    wait=list(wait_x) + [s_ve.at()])
            wxq = (s_dg if cast else s_dw, wx)
            if compute_stats:
                pr.do("vector", lambda e: e.tensor_reduce(
                    sums_sb[:, 0:1], x_f[:], AX.X, ALU.add),
                    wait=[wxq], inc=s_ve)
                pr.do("vector", lambda e: e.scalar_tensor_tensor(
                    out=ch_f[:, 0:D], in0=x_f[:], scalar=1.0, in1=x_f[:],
                    op0=ALU.mult, op1=ALU.mult, accum_out=sums_sb[:, 1:2]), inc=s_ve)
                pr.do("vector", lambda e, c=ch, m=m_st: e.tensor_scalar_mul(
                    m[:, c:c + 1], sums_sb[:, 0:1], 1.0 / D),
                    wait=[s_ve.at()], inc=s_ve)
                pr.do("vector", lambda e, c=ch, m=m_st: e.scalar_tensor_tensor(
                    out=sums_sb[:, 2:3], in0=m[:, c:c + 1], scalar=-1.0,
                    in1=m[:, c:c + 1], op0=ALU.mult, op1=ALU.mult),
                    wait=[s_ve.at()], inc=s_ve)
                pr.do("vector", lambda e: e.scalar_tensor_tensor(
                    out=sums_sb[:, 3:4], in0=sums_sb[:, 1:2], scalar=1.0 / D,
                    in1=sums_sb[:, 2:3], op0=ALU.mult, op1=ALU.add),
                    wait=[s_ve.at()], inc=s_ve)
                v_var = s_ve.n
                pr.do("scalar", lambda e: e.activation(
                    out=sums_sb[:, 4:5], in_=sums_sb[:, 3:4], func=AF.Ln,
                    bias=eps1_sb[:]), wait=[(s_ve, v_var)], inc=s_sc)
                pr.do("scalar", lambda e, c=ch, r=rs_st: e.activation(
                    out=r[:, c:c + 1], in_=sums_sb[:, 4:5], func=AF.Exp, scale=-0.5),
                    inc=s_sc)
            sc_rs = s_sc.n
            pr.do("vector", lambda e, c=ch, m=m_st, r=rs_st: e.tensor_scalar(
                out=xln_tok[:], in0=x_f[:], scalar1=m[:, c:c + 1],
                scalar2=r[:, c:c + 1], op0=ALU.subtract, op1=ALU.mult),
                wait=[(s_sc, sc_rs), wxq, s_te.at()], inc=s_ve)
            v_xln = s_ve.n
            for k in range(KD):
                b = k % 2
                pr.do("tensor", lambda e, k=k, b=b: e.transpose(
                    pt[b][:, 0:128], xln_tok[:, k * 128:(k + 1) * 128], eye_sb[:]),
                    wait=[(s_ve, v_xln), (s_sc, pt_evac[b]), (s_gp, g_consts)],
                    inc=s_te)
                wt = s_te.n
                pr.do("scalar", lambda e, k=k, c=cc, b=b: e.copy(
                    out=xln_fm[:, k * SC + c * 128:k * SC + (c + 1) * 128],
                    in_=pt[b][:, 0:128]),
                    wait=[(s_te, wt), (s_dg, conv_done), (s_te, convT_te),
                          (s_out, convT_out)] + list(fm_free),
                    inc=s_sc)
                pt_evac[b] = s_sc.n
        return s_sc.n

    stg_free = [0, 0, 0, 0]

    def emit_tokmajor_matmuls(sc, src_w, col0, ncols, dst_sp, xln_done):
        for cg in range(ncols // 512):
            wl = dsync(wk_bf[:].rearrange("p (k c) -> p k c", k=KD),
                       src_w[:, col0 + cg * 512:col0 + (cg + 1) * 512]
                       .rearrange("(k p) c -> p k c", p=128),
                       wait=[(s_cc, cc_zxs), s_te.at(), (s_out, convT_out)])
            for tt in range(CPS):
                psum = ps[tt % 2]
                for k in range(KD):
                    pr.do("tensor", lambda e, k=k, tt=tt, p=psum:
                          e.matmul(p[:, 0:512],
                                   xln_fm[:, k * SC + tt * 128:k * SC + (tt + 1) * 128],
                                   wk_bf[:, k * 512:(k + 1) * 512],
                                   start=(k == 0), stop=(k == KD - 1)),
                          wait=[(s_dw, wl), (s_sc, xln_done)] + psf[tt % 2],
                          inc=s_te)
                wt = s_te.n
                pr.do("scalar", lambda e, tt=tt, p=psum: e.copy(
                    out=stg_sb[:, tt * 512:(tt + 1) * 512], in_=p[:, 0:512]),
                    wait=[(s_te, wt), (s_out, stg_free[tt])], inc=s_sc)
                psf[tt % 2] = [s_sc.at()]
                wsc = s_sc.n
                stg_free[tt] = dout(
                    dst_sp[(sc * CPS + tt) * 128:(sc * CPS + tt + 1) * 128,
                           cg * 512:(cg + 1) * 512],
                    stg_sb[:, tt * 512:(tt + 1) * 512], wait=[(s_sc, wsc)])

    for sc in range(NSC if stage >= 0 else 0):
        xln_done = emit_xln_superchunk(sc, m1_sb, rs1_sb, x_in, True)
        if stage < 1:
            continue
        for m in range(2):
            mw, moff = (128, 0) if m == 0 else (64, 128)
            psum = ps[4]
            for k in range(KD):
                pr.do("tensor", lambda e, k=k, mw=mw, moff=moff, p=psum:
                      e.matmul(p[0:mw, 0:SC],
                               wdtbc_sb[:, k * 192 + moff:k * 192 + moff + mw],
                               xln_fm[:, k * SC:(k + 1) * SC],
                               start=(k == 0), stop=(k == KD - 1)),
                      wait=[(s_sc, xln_done), (s_dg, g_dtbc)] + psf[4],
                      inc=s_te)
            wt = s_te.n
            if m == 0:
                pr.do("scalar", lambda e, s0=sc, p=psum: e.copy(
                    out=braw_sb[:, s0 * SC:(s0 + 1) * SC], in_=p[0:64, 0:SC]),
                    wait=[(s_te, wt)], inc=s_sc)
                pr.do("scalar", lambda e, s0=sc, p=psum: e.copy(
                    out=craw_sb[:, s0 * SC:(s0 + 1) * SC], in_=p[64:128, 0:SC]),
                    inc=s_sc)
            else:
                pr.do("scalar", lambda e, s0=sc, p=psum: e.copy(
                    out=dt_sb[:, s0 * SC:(s0 + 1) * SC], in_=p[0:64, 0:SC]),
                    wait=[(s_te, wt)], inc=s_sc)
            psf[4] = [s_sc.at()]
        emit_tokmajor_matmuls(sc, cc_zxs_o, DI, DI, xs_sp, xln_done)
    ln1_done_sc = s_sc.n
    xs_spilled = s_out.n

    if debug and stage >= 1:
        d2 = mkdbg("dbg_braw", [NS, T])
        dout(d2[:], braw_sb[:], wait=[(s_sc, ln1_done_sc)])
        d2b = mkdbg("dbg_xs", [256, DI], BF16)
        dout(d2b[:], xs_sp[0:256, :], wait=[(s_out, s_out.n)])

    if stage >= 2:
        pr.do("scalar", lambda e: e.activation(out=dt_sb[:], in_=dt_sb[:],
              func=AF.Exp, bias=dtb_sb[:]), wait=[(s_sc, ln1_done_sc), (s_dw, w_small)],
              inc=s_sc)
        pr.do("scalar", lambda e: e.activation(out=dt_sb[:], in_=dt_sb[:],
              func=AF.Ln, bias=1.0), inc=s_sc)
        sc_dt = s_sc.n
        pr.do("vector", lambda e: e.tensor_reduce(sums_sb[0:NH, 5:6], dt_sb[:], AX.X, ALU.add),
              wait=[(s_sc, sc_dt)], inc=s_ve)
        v_dts = s_ve.n
        dgps(cc_dts_i[:], sums_sb[0:NH, 5:6], wait=[(s_ve, v_dts)])
        g_dts = s_dg.n
        pr.do("gpsimd", lambda e: e.collective_compute(
            "AllGather", ALU.bypass,
            replica_groups=[[2 * i, 2 * i + 1] for i in range(4)],
            ins=[cc_dts_i.ap().opt()], outs=[cc_dts_o.ap().opt()]),
            wait=[(s_dg, g_dts)], inc=s_cc)
        cc_dts = s_cc.n
        w_off = dsync(dts_off[:], cc_dts_o[0:NH, :], wait=[(s_cc, cc_dts)])
        pr.do("vector", lambda e: e.tensor_scalar_mul(
            dts_off[:], dts_off[:], maskrep[0:NH, :]),
            wait=[(s_dw, w_off), (s_sc, sc_mask)], inc=s_ve)
        v_off = s_ve.n
        pr.do("vector", lambda e: e.memset(etmp_sb[:], 0.0), inc=s_ve)
        pr.do("vector", lambda e: e.tensor_tensor_scan(
            out=cum_sb[:], data0=dt_sb[:], data1=etmp_sb[:, 0:1].broadcast_to([NH, T]),
            initial=dts_off[:], op0=ALU.add, op1=ALU.add),
            wait=[(s_ve, v_off)], inc=s_ve)
        pr.do("vector", lambda e: e.tensor_scalar_mul(
            cuml_sb[:], cum_sb[:], nega_sb[:]), wait=[(s_ve, v_nega)], inc=s_ve)
        pr.do("vector", lambda e: e.tensor_mul(cl0_sb[:], dts_off[:], nega_sb[:]), inc=s_ve)
        v_cum = s_ve.n
        pr.do("vector", lambda e: e.tensor_copy(out=chend_sb[:, 0:1], in_=cl0_sb[:]),
              wait=[(s_ve, v_cum)], inc=s_ve)
        pr.do("vector", lambda e: e.tensor_copy(
            out=chend_sb[:, 1:NCH + 1].rearrange("h (c o) -> h c o", o=1),
            in_=cuml_sb[:].rearrange("h (c l) -> h c l", l=L)[:, :, L - 1:L]),
            inc=s_ve)
        v_chend = s_ve.n
        pr.do("vector", lambda e: e.tensor_scalar(
            out=dall_sb[:], in0=chend_sb[:, 0:NCH], scalar1=cl0_sb[:], scalar2=None,
            op0=ALU.subtract), wait=[(s_ve, v_chend)], inc=s_ve)
        pr.do("vector", lambda e: e.tensor_sub(
            etot_sb[:], chend_sb[:, 1:NCH + 1], chend_sb[:, 0:NCH]), inc=s_ve)
        v_de = s_ve.n
        pr.do("scalar", lambda e: e.activation(out=dall_sb[:], in_=dall_sb[:], func=AF.Exp),
              wait=[(s_ve, v_de)], inc=s_sc)
        pr.do("scalar", lambda e: e.activation(out=etot_sb[:], in_=etot_sb[:], func=AF.Exp),
              inc=s_sc)
        sc_de = s_sc.n
        sc_stage2 = s_sc.n

    if debug and stage >= 2:
        d3 = mkdbg("dbg_dt", [NH, T])
        d4 = mkdbg("dbg_cum", [NH, T])
        dout(d3[:], dt_sb[:], wait=[(s_sc, s_sc.n), (s_ve, s_ve.n)])
        dout(d4[:], cum_sb[:], wait=[(s_ve, s_ve.n)])


    sc_a2 = s_sc.n
    cc_h = s_cc.n
    v_hin = s_ve.n
    if stage >= 3:
        eyef64 = eyef_sb[64:128, 64:128]
        eyef0 = eyef_sb[0:64, 0:64]

        g_trig_sp = 0
        for k in range(NCH):
            pr.do("tensor", lambda e, k=k: e.transpose(
                ps[4][:, 0:64], cum_sb[:, k * L:(k + 1) * L], eyef64),
                wait=[(s_ve, v_cum)] + psf[4], inc=s_te)
            wt = s_te.n
            pr.do("scalar", lambda e: e.copy(out=tp_sb[:, 0:64], in_=ps[4][:, 0:64]),
                  wait=[(s_te, wt), s_ve.at()], inc=s_sc)
            psf[4] = [s_sc.at()]
            sc_tp = s_sc.n
            pr.do("vector", lambda e: e.tensor_tensor(
                ch_f[:, 0:2048].rearrange("p (h f) -> p h f", f=32),
                tp_sb[:, 0:64].rearrange("p (h o) -> p h o", o=1)
                .broadcast_to([128, 64, 32]),
                freq2_v[:].rearrange("p (h f) -> p h f", f=32), ALU.mult),
                wait=[s_sc.at()], inc=s_ve)
            for half in range(2):
                h0 = half * 1024
                for trig in range(2):
                    if trig == 1:
                        pr.do("vector", lambda e, h0=h0: e.tensor_scalar_add(
                            ch_f[:, h0:h0 + 1024], ch_f[:, h0:h0 + 1024], 0.75),
                            wait=[s_sc.at()], inc=s_ve)
                    v_t = s_ve.n
                    pr.do("scalar", lambda e, h0=h0: e.copy(
                        out=phii_sb[:], in_=ch_f[:, h0:h0 + 1024]),
                        wait=[(s_ve, v_t)], inc=s_sc)
                    sc_c1 = s_sc.n
                    pr.do("vector", lambda e, h0=h0: e.tensor_sub(
                        ch_f[:, h0:h0 + 1024], ch_f[:, h0:h0 + 1024], phii_sb[:]),
                        wait=[(s_sc, sc_c1)], inc=s_ve)
                    v_r = s_ve.n
                    pr.do("scalar", lambda e, h0=h0: e.activation(
                        out=phii_sb[:], in_=ch_f[:, h0:h0 + 1024], func=AF.Copy,
                        bias=-0.5, scale=1.0), wait=[(s_ve, v_r)], inc=s_sc)
                    sc_c2 = s_sc.n
                    pr.do("vector", lambda e, h0=h0: e.scalar_tensor_tensor(
                        out=ch_f[:, h0:h0 + 1024], in0=ch_f[:, h0:h0 + 1024],
                        scalar=-0.5, in1=phii_sb[:], op0=ALU.add, op1=ALU.subtract),
                        wait=[(s_sc, sc_c2)], inc=s_ve)
                    v_fr = s_ve.n
                    off = (1 - trig) * 2048 + h0
                    pr.do("scalar", lambda e, o=off, h0=h0: e.activation(
                        out=trig_v[:, o:o + 1024], in_=ch_f[:, h0:h0 + 1024],
                        func=AF.Sin, scale=-TWO_PI),
                        wait=[(s_ve, v_fr), (s_dg, g_trig_sp)], inc=s_sc)
            sc_trig = s_sc.n
            g_trig_sp = dgps(trig_sp[k, :, :], trig_v[:], wait=[(s_sc, sc_trig)])
        g_trig = s_dg.n

        pr.do("vector", lambda e: e.memset(hrun_sb[:], 0.0), inc=s_ve)
        stg2_free = {id(bt_sb): [0, 0], id(ct_sb): [0, 0]}
        for k in range(NCH):
            w_xs = dsync(ch_d[:], xs_sp[k * L:(k + 1) * L, :],
                         wait=[(s_out, xs_spilled), s_ve.at(), s_te.at()])
            w_tr = dsync(trig_v[:], trig_sp[k, :, :],
                         wait=[(s_dg, g_trig), s_sc.at()])
            pr.do("scalar", lambda e, k=k: e.activation(
                out=etmp_sb[:], in_=cuml_sb[:, k * L:(k + 1) * L], func=AF.Exp,
                scale=-1.0, bias=chend_sb[:, k + 1:k + 2]),
                wait=[(s_sc, sc_stage2)], inc=s_sc)
            sc_E = s_sc.n
            tps = [
                (dt_sb[:, k * L:(k + 1) * L], 0, eyef0, [(s_sc, sc_dt)]),
                (etmp_sb[:], 64, eyef0, [(s_sc, sc_E)]),
                (braw_sb[:, k * L:(k + 1) * L], 128, eyef0, [(s_sc, ln1_done_sc)]),
                (craw_sb[:, k * L:(k + 1) * L], 192, eyef64, [(s_sc, ln1_done_sc)]),
                (etot_sb[:, k:k + 1], 256, eyef0, [(s_sc, sc_stage2)]),
            ]
            for (src_ap, col, ident, extraw) in tps:
                pr.do("tensor", lambda e, s=src_ap, c=col, idn=ident:
                      e.transpose(ps[4][0:s.shape[-1], c:c + s.shape[0]], s, idn),
                      wait=extraw + psf[4], inc=s_te)
            wt = s_te.n
            pr.do("scalar", lambda e: e.copy(out=tp_sb[:, 0:256], in_=ps[4][:, 0:256]),
                  wait=[(s_te, wt)], inc=s_sc)
            pr.do("scalar", lambda e: e.copy(out=tp_sb[0:1, 392:456],
                  in_=ps[4][0:1, 256:320]), inc=s_sc)
            psf[4] = [s_sc.at()]
            sc_tp = s_sc.n
            for (roff, dcol) in ((128, 384), (192, 385)):
                pr.do("vector", lambda e, ro=roff, dc=dcol: e.scalar_tensor_tensor(
                    out=tp_sb[:, 320:384], in0=tp_sb[:, ro:ro + 64], scalar=1.0,
                    in1=tp_sb[:, ro:ro + 64], op0=ALU.mult, op1=ALU.mult,
                    accum_out=tp_sb[:, dc:dc + 1]),
                    wait=[(s_sc, sc_tp), s_ve.at()], inc=s_ve)
                v_sq = s_ve.n
                pr.do("scalar", lambda e, dc=dcol: e.activation(
                    out=tp_sb[:, dc:dc + 1], in_=tp_sb[:, dc:dc + 1], func=AF.Ln,
                    scale=1.0 / NS, bias=eps2_sb[:]), wait=[(s_ve, v_sq)], inc=s_sc)
                pr.do("scalar", lambda e, dc=dcol: e.activation(
                    out=tp_sb[:, dc:dc + 1], in_=tp_sb[:, dc:dc + 1], func=AF.Exp,
                    scale=-0.5), inc=s_sc)
            sc_rms = s_sc.n
            pr.do("vector", lambda e: e.tensor_mul(
                tp_sb[:, 256:320], tp_sb[:, 0:64], tp_sb[:, 64:128]),
                wait=[(s_sc, sc_tp)], inc=s_ve)
            pr.do("vector", lambda e: e.tensor_tensor(
                ch_c[:].rearrange("p (h q) -> p h q", q=PH),
                ch_d[:].rearrange("p (h q) -> p h q", q=PH),
                tp_sb[:, 256:320].rearrange("p (h o) -> p h o", o=1)
                .broadcast_to([128, NH, PH]), ALU.mult),
                wait=[(s_dw, w_xs)], inc=s_ve)
            for (dst, rsc_c, bias_t, roff) in ((ch_a, 384, biasb_v, 128),
                                               (ch_b, 385, biasc_v, 192)):
                for xh in range(2):
                    pr.do("vector", lambda e, d=dst, rc=rsc_c, bt=bias_t, ro=roff,
                          xh=xh: e.scalar_tensor_tensor(
                              out=d[:].rearrange("p (h f) -> p h f", f=64)
                              [:, :, xh * 32:(xh + 1) * 32],
                              in0=tp_sb[:, ro:ro + 64][:, xh::2]
                              .rearrange("p (o f) -> p o f", o=1)
                              .broadcast_to([128, NH, 32]),
                              scalar=tp_sb[:, rc:rc + 1],
                              in1=bt[:].rearrange("p (h f) -> p h f", f=64)
                              [:, :, xh * 32:(xh + 1) * 32],
                              op0=ALU.mult, op1=ALU.add),
                          wait=[(s_sc, sc_bcasts), (s_sc, sc_rms)], inc=s_ve)
            v_pre = s_ve.n

            def hv(t, half):
                return (t[:].rearrange("p (h x f) -> p h x f", x=2, f=32)
                        [:, :, half:half + 1, :]
                        .rearrange("p h o f -> p h (o f)"))

            cosap = trig_v[:, 0:2048].rearrange("p (h f) -> p h f", f=32)
            sinap = trig_v[:, 2048:4096].rearrange("p (h f) -> p h f", f=32)
            q1 = chtmp_v[:, 0:2048].rearrange("p (h f) -> p h f", f=32)
            q2 = chtmp_v[:, 2048:4096].rearrange("p (h f) -> p h f", f=32)
            for dst in (ch_a, ch_b):
                v1, v2 = hv(dst, 0), hv(dst, 1)
                pr.do("vector", lambda e, v1=v1: e.tensor_tensor(
                    q1, v1, cosap, ALU.mult),
                    wait=[(s_dw, w_tr), (s_ve, v_pre)], inc=s_ve)
                pr.do("vector", lambda e, v2=v2: e.tensor_tensor(
                    q2, v2, sinap, ALU.mult), inc=s_ve)
                pr.do("vector", lambda e, v1=v1: e.tensor_tensor(
                    v1, v1, sinap, ALU.mult), inc=s_ve)
                pr.do("vector", lambda e, v2=v2: e.tensor_tensor(
                    v2, v2, cosap, ALU.mult), inc=s_ve)
                pr.do("vector", lambda e, v1=v1, v2=v2: e.tensor_add(
                    v2, v1, v2), inc=s_ve)
                pr.do("vector", lambda e, v1=v1: e.tensor_sub(
                    v1, q1, q2), inc=s_ve)
            v_rope = s_ve.n
            for (srcb, spt, stgbuf) in ((ch_a, bt_sp, bt_sb), (ch_b, ct_sp, ct_sb)):
                for g in range(8):
                    b = g % 2
                    for jj in range(4):
                        j = g * 4 + jj
                        pr.do("tensor", lambda e, s=srcb, j=j, b=b, jj=jj:
                              e.transpose(pt[b][:, jj * 128:(jj + 1) * 128],
                                          s[:, j * 128:(j + 1) * 128], eye_sb[:]),
                              wait=[(s_ve, v_rope), (s_sc, pt_evac[b])], inc=s_te)
                    wt = s_te.n
                    half = (g % 2) * 512
                    pr.do("scalar", lambda e, b=b, hf=half, sg=stgbuf: e.copy(
                        out=sg[:, hf:hf + 512], in_=pt[b][:, 0:512]),
                        wait=[(s_te, wt), (s_out, stg2_free[id(stgbuf)][g % 2])],
                        inc=s_sc)
                    pt_evac[b] = s_sc.n
                    wsc = s_sc.n
                    stg2_free[id(stgbuf)][g % 2] = dout(
                        spt[k, g * 4:(g + 1) * 4, :, :].rearrange("j p t -> p j t"),
                        stgbuf[:, half:half + 512].rearrange("p (j t) -> p j t", j=4),
                        wait=[(s_sc, wsc)])
            for h in range(NH):
                bi = h // 16
                col = ((h % 16) // 2) * 64
                base = (h % 2) * 64
                pr.do("tensor", lambda e, h=h, bi=bi, c=col, b=base: e.matmul(
                    ps[bi][b:b + 64, c:c + 64], ch_a[:, h * PH:(h + 1) * PH],
                    ch_c[:, h * PH:(h + 1) * PH], start=True, stop=True,
                    tile_position=(0, b)),
                    wait=[(s_ve, v_rope)] + psf[h // 16], inc=s_te)
            te_s = s_te.n
            g_hspill = dgps(hloc_sp[k, :, :], hrun_sb[:], wait=[s_ve.at()])
            for j in range(4):
                pr.do("tensor", lambda e, j=j: e.matmul(
                    ps[4][0:64, 0:512], ones1f[0:1, 0:64],
                    tp_sb[0:1, 392:456][:, 0::2][:, j * 8:(j + 1) * 8]
                    .rearrange("o (q z) -> o q z", z=1).broadcast_to([1, 8, 64]),
                    start=True, stop=True),
                    wait=[(s_sc, sc_tp)] + psf[4], inc=s_te)
                pr.do("tensor", lambda e, j=j: e.matmul(
                    ps[4][64:128, 0:512], ones1f[0:1, 0:64],
                    tp_sb[0:1, 392:456][:, 1::2][:, j * 8:(j + 1) * 8]
                    .rearrange("o (q z) -> o q z", z=1).broadcast_to([1, 8, 64]),
                    start=True, stop=True, tile_position=(0, 64)), inc=s_te)
                te_e = s_te.n
                pr.do("vector", lambda e, j=j: e.tensor_tensor(
                    hrun_sb[:, j * 512:(j + 1) * 512],
                    hrun_sb[:, j * 512:(j + 1) * 512], ps[4][:, 0:512], ALU.mult),
                    wait=[(s_te, te_e), (s_dg, g_hspill)], inc=s_ve)
                psf[4] = [s_ve.at()]
                pr.do("vector", lambda e, j=j: e.tensor_tensor(
                    hrun_sb[:, j * 512:(j + 1) * 512],
                    hrun_sb[:, j * 512:(j + 1) * 512], ps[j][:, 0:512], ALU.add),
                    wait=[(s_te, te_s)], inc=s_ve)
                psf[j] = [s_ve.at()]
        v_hfin = s_ve.n
        g_hout = dgps(cc_h_i[:], hrun_sb[:], wait=[(s_ve, v_hfin)])
        pr.do("gpsimd", lambda e: e.collective_compute(
            "AllGather", ALU.bypass,
            replica_groups=[[2 * i, 2 * i + 1] for i in range(4)],
            ins=[cc_h_i.ap().opt()], outs=[cc_h_o.ap().opt()]),
            wait=[(s_dg, g_hout)], inc=s_cc)
        cc_h = s_cc.n
        w_hin = dsync(hrun_sb[:], cc_h_o[0:128, :], wait=[(s_cc, cc_h), s_ve.at()])
        pr.do("vector", lambda e: e.tensor_scalar_mul(
            hin_sb[:], hrun_sb[:], maskrep[:]),
            wait=[(s_dw, w_hin), (s_sc, sc_mask)], inc=s_ve)
        v_hin = s_ve.n
        sc_a2 = s_sc.n
        a2_spills = s_out.n

    cc_wo = s_cc.n
    cc_w1 = cc_w2 = cc_w3 = s_cc.n
    if stage >= 5:
        pr.do("gpsimd", lambda e: e.collective_compute(
            "AllGather", ALU.bypass, replica_groups=[list(range(NCORES))],
            ins=[cc_wo_i.ap().opt()], outs=[cc_wo_o.ap().opt()]),
            wait=[(s_dg, g_zxs)], inc=s_cc)
        cc_wo = s_cc.n
    if stage >= 6:
        for (ci, co) in ((cc_w1_i, cc_w1_o), (cc_w2_i, cc_w2_o), (cc_w3_i, cc_w3_o)):
            pr.do("gpsimd", lambda e, ci=ci, co=co: e.collective_compute(
                "AllGather", ALU.bypass, replica_groups=[list(range(NCORES))],
                ins=[ci.ap().opt()], outs=[co.ap().opt()]),
                wait=[(s_dg, g_zxs), (s_out, convT_out)], inc=s_cc)
        cc_w3 = s_cc.n

    if debug and stage >= 3:
        d5 = mkdbg("dbg_hfin", [128, 32 * PH])
        dout(d5[:], cc_h_i[:], wait=[(s_dg, s_dg.n)])
        d6 = mkdbg("dbg_bh", [128, DI], BF16)
        dout(d6[:].rearrange("p (j t) -> p j t", j=32),
             bt_sp[NCH - 1, :, :, :].rearrange("j p t -> p j t"),
             wait=[(s_out, s_out.n)])
        d6b = mkdbg("dbg_ch", [128, DI], BF16)
        dout(d6b[:].rearrange("p (j t) -> p j t", j=32),
             ct_sp[NCH - 1, :, :, :].rearrange("j p t -> p j t"),
             wait=[(s_out, s_out.n)])


    b_done = s_out.n
    if stage >= 5:
        w_nw = dgps(nrmw_v[0:1, :], norm_in[:], wait=[(s_ve, v_hin)])
        bcast_row(nrmw_v, nrmw_v, 128, [(s_dg, w_nw), (s_ve, v_hin)])
        w_dp = dsync(dp_sb[:], d_param[:])
        for b8 in range(8):
            pr.do("tensor", lambda e, b8=b8: e.matmul(
                ps[5][:, 0:512], ones1f[0:1, 0:128],
                dp_sb[0:1, b8 * 8:(b8 + 1) * 8]
                .rearrange("o (q z) -> o q z", z=1).broadcast_to([1, 8, 64]),
                start=True, stop=True),
                wait=[(s_dw, w_dp), (s_ve, v_hin), s_sc.at()] + psf[5], inc=s_te)
            wt = s_te.n
            pr.do("scalar", lambda e, b8=b8: e.copy(
                out=drep_v[:, b8 * 512:(b8 + 1) * 512], in_=ps[5][:, 0:512]),
                wait=[(s_te, wt)], inc=s_sc)
            psf[5] = [s_sc.at()]
        sc_drep = s_sc.n
        pr.do("vector", lambda e: e.tensor_scalar_mul(
            ch_f[0:64, :], cuml_sb[:], -1.0), wait=[(s_ve, v_hin)], inc=s_ve)
        v_ncl = s_ve.n
        g_cl0 = dgps(clch_sp[0, :, :], ch_f[0:64, :], wait=[(s_ve, v_ncl)])
        g_cl = dgps(clch_sp[1, :, :], cuml_sb[:], wait=[(s_ve, v_cum)])
        pr.do("vector", lambda e: e.memset(dfl_sb[0:2, :], 1.0), inc=s_ve)
        pr.do("vector", lambda e: e.memset(dfr_sb[0:2, :], 1.0), inc=s_ve)
        v_dfones = s_ve.n

        for sc in range(NSC):
            xd = emit_xln_superchunk(sc, m1_sb, rs1_sb, x_in, False)
            emit_tokmajor_matmuls(sc, cc_zxs_o, 0, DI, z_sp, xd)
            z_done = s_out.n
            for cci in range(CPS):
                k = sc * CPS + cci
                w_xs = dsync(ch_d[:], xs_sp[k * L:(k + 1) * L, :],
                             wait=[(s_out, xs_spilled), s_ve.at()])
                w_z = dsync(ch_b[:], z_sp[k * L:(k + 1) * L, :],
                            wait=[(s_out, z_done), s_sc.at(), s_ve.at()])
                w_hl = dsync(hlr_v[:], hloc_sp[k, :, :],
                             wait=[(s_dg, g_hout), s_ve.at(), s_te.at()])
                pr.do("vector", lambda e, k=k: e.tensor_scalar(
                    out=etmp_sb[:], in0=cuml_sb[:, k * L:(k + 1) * L],
                    scalar1=chend_sb[:, k:k + 1], scalar2=None, op0=ALU.subtract),
                    wait=[s_sc.at()], inc=s_ve)
                v_es = s_ve.n
                pr.do("scalar", lambda e: e.activation(
                    out=etmp_sb[:], in_=etmp_sb[:], func=AF.Exp),
                    wait=[(s_ve, v_es)], inc=s_sc)
                sc_esrt = s_sc.n
                tpsb = [
                    (dt_sb[:, k * L:(k + 1) * L], 0, eyef0, [(s_sc, sc_dt)]),
                    (etmp_sb[:], 64, eyef0, [(s_sc, sc_esrt)]),
                    (dall_sb[:, k:k + 1], 256, eyef0, [(s_sc, sc_stage2)]),
                ]
                for (src_ap, col, ident, extraw) in tpsb:
                    pr.do("tensor", lambda e, s=src_ap, c=col, idn=ident:
                          e.transpose(ps[4][0:s.shape[-1], c:c + s.shape[0]], s, idn),
                          wait=extraw + psf[4], inc=s_te)
                wt = s_te.n
                pr.do("scalar", lambda e: e.copy(
                    out=tp_sb[:, 0:128], in_=ps[4][:, 0:128]),
                    wait=[(s_te, wt)], inc=s_sc)
                pr.do("scalar", lambda e: e.copy(
                    out=tp_sb[0:1, 392:456], in_=ps[4][0:1, 256:320]), inc=s_sc)
                psf[4] = [s_sc.at()]
                sc_tp = s_sc.n
                pr.do("vector", lambda e: e.tensor_tensor(
                    ch_c[:].rearrange("p (h q) -> p h q", q=PH),
                    ch_d[:].rearrange("p (h q) -> p h q", q=PH),
                    tp_sb[:, 0:64].rearrange("p (h o) -> p h o", o=1)
                    .broadcast_to([128, NH, PH]), ALU.mult),
                    wait=[(s_dw, w_xs), (s_sc, sc_tp)], inc=s_ve)
                for j in range(4):
                    pr.do("tensor", lambda e, j=j: e.matmul(
                        ps[4][0:64, 0:512], ones1f[0:1, 0:64],
                        tp_sb[0:1, 392:456][:, 0::2][:, j * 8:(j + 1) * 8]
                        .rearrange("o (q z) -> o q z", z=1).broadcast_to([1, 8, 64]),
                        start=True, stop=True),
                        wait=[(s_sc, sc_tp)] + psf[4], inc=s_te)
                    pr.do("tensor", lambda e, j=j: e.matmul(
                        ps[4][64:128, 0:512], ones1f[0:1, 0:64],
                        tp_sb[0:1, 392:456][:, 1::2][:, j * 8:(j + 1) * 8]
                        .rearrange("o (q z) -> o q z", z=1).broadcast_to([1, 8, 64]),
                        start=True, stop=True, tile_position=(0, 64)), inc=s_te)
                    te_e = s_te.n
                    pr.do("vector", lambda e, j=j: e.tensor_tensor(
                        hk_v[:, j * 512:(j + 1) * 512], hin_sb[:, j * 512:(j + 1) * 512],
                        ps[4][:, 0:512], ALU.mult),
                        wait=[(s_te, te_e), (s_ve, v_hin), (s_dw, w_hl)], inc=s_ve)
                    psf[4] = [s_ve.at()]
                    pr.do("vector", lambda e, j=j: e.tensor_add(
                        hk_v[:, j * 512:(j + 1) * 512], hk_v[:, j * 512:(j + 1) * 512],
                        hlr_v[:, j * 512:(j + 1) * 512]), inc=s_ve)
                v_hk = s_ve.n
                for hg in range(8):
                    hb = hg % 2
                    w_bt = dsync(bt_sb[:, hb * 512:(hb + 1) * 512]
                                 .rearrange("p (j t) -> p j t", j=4),
                                 bt_sp[k, hg * 4:(hg + 1) * 4, :, :]
                                 .rearrange("j p t -> p j t"),
                                 wait=[(s_out, a2_spills), s_te.at()])
                    w_ct = dsync(ct_sb[:, hb * 512:(hb + 1) * 512]
                                 .rearrange("p (j t) -> p j t", j=4),
                                 ct_sp[k, hg * 4:(hg + 1) * 4, :, :]
                                 .rearrange("j p t -> p j t"),
                                 wait=[(s_out, a2_spills), s_te.at()])
                    w_dfl = dsync(dfl_sb[0:1, :], clch_sp[0, hg * 8:(hg + 1) * 8,
                                                          k * L:(k + 1) * L],
                                  wait=[(s_dg, g_cl0), s_te.at()])
                    w_dfr = dsync(dfr_sb[1:2, :], clch_sp[1, hg * 8:(hg + 1) * 8,
                                                          k * L:(k + 1) * L],
                                  wait=[(s_dg, g_cl), s_te.at()])
                    for hh in range(8):
                        h = hg * 8 + hh
                        ji = hh // 2
                        base = (h % 2) * 64
                        gb = ps[hh // 4]
                        pr.do("tensor", lambda e, hh=hh, ji=ji, b=base, gb=gb, hb=hb:
                              e.matmul(gb[:, (hh % 4) * 128:(hh % 4 + 1) * 128],
                                       bt_sb[b:b + 64,
                                             hb * 512 + ji * 128:hb * 512 + (ji + 1) * 128],
                                       ct_sb[b:b + 64,
                                             hb * 512 + ji * 128:hb * 512 + (ji + 1) * 128],
                                       start=True, stop=True),
                              wait=[(s_dw, max(w_bt, w_ct))] + psf[hh // 4], inc=s_te)
                        db = ps[2 + hh // 4]
                        pr.do("tensor", lambda e, hh=hh, db=db: e.matmul(
                            db[:, (hh % 4) * 128:(hh % 4 + 1) * 128],
                            dfl_sb[:, hh * 128:(hh + 1) * 128],
                            dfr_sb[:, hh * 128:(hh + 1) * 128],
                            start=True, stop=True),
                            wait=[(s_dw, max(w_dfl, w_dfr)), (s_ve, v_dfones)]
                            + psf[2 + hh // 4], inc=s_te)
                    te_gd = s_te.n
                    gmh = gm_v[:, (hg % 2) * 1024:(hg % 2) * 1024 + 1024]
                    for bb in range(2):
                        pr.do("scalar", lambda e, bb=bb: e.activation(
                            out=ch_f[:, bb * 512:(bb + 1) * 512], in_=ps[2 + bb][:, 0:512],
                            func=AF.Relu, scale=-1.0), wait=[(s_te, te_gd)], inc=s_sc)
                        pr.do("scalar", lambda e, bb=bb, g=gmh: e.activation(
                            out=g[:, bb * 512:(bb + 1) * 512],
                            in_=ch_f[:, bb * 512:(bb + 1) * 512],
                            func=AF.Exp, scale=-1.0), inc=s_sc)
                        sc_me = s_sc.n
                        pr.do("vector", lambda e, bb=bb, g=gmh: e.tensor_tensor(
                            g[:, bb * 512:(bb + 1) * 512]
                            .rearrange("p (j t) -> p j t", j=4),
                            g[:, bb * 512:(bb + 1) * 512]
                            .rearrange("p (j t) -> p j t", j=4),
                            tri_sb[:].rearrange("p (o t) -> p o t", o=1)
                            .broadcast_to([128, 4, 128]), ALU.mult),
                            wait=[(s_sc, sc_me), (s_gp, g_consts)], inc=s_ve)
                        pr.do("vector", lambda e, bb=bb, g=gmh: e.tensor_tensor(
                            g[:, bb * 512:(bb + 1) * 512],
                            g[:, bb * 512:(bb + 1) * 512],
                            ps[bb][:, 0:512], ALU.mult), inc=s_ve)
                        psf[bb] = [s_ve.at()]
                        psf[2 + bb] = [s_ve.at()]
                    v_gm = s_ve.n
                    for hh in range(8):
                        h = hg * 8 + hh
                        base = (h % 2) * 64
                        pr.do("tensor", lambda e, h=h, hh=hh, g=gmh: e.matmul(
                            ps[4][:, hh * 64:(hh + 1) * 64],
                            g[:, hh * 128:(hh + 1) * 128],
                            ch_c[:, h * PH:(h + 1) * PH], start=True, stop=True),
                            wait=[(s_ve, v_gm)] + psf[4], inc=s_te)
                        pr.do("tensor", lambda e, h=h, hh=hh, b=base, hb=hb: e.matmul(
                            ps[5][:, hh * 64:(hh + 1) * 64],
                            ct_sb[b:b + 64,
                                  hb * 512 + (hh // 2) * 128:hb * 512 + (hh // 2 + 1) * 128],
                            hk_v[b:b + 64, (h // 2) * 64:(h // 2 + 1) * 64],
                            start=True, stop=True),
                            wait=[(s_ve, v_hk)] + psf[5], inc=s_te)
                    te_y = s_te.n
                    pr.do("vector", lambda e, hg=hg: e.tensor_tensor(
                        ch_e[:, hg * 512:(hg + 1) * 512]
                        .rearrange("p (h q) -> p h q", q=PH),
                        ps[5][:, 0:512].rearrange("p (h q) -> p h q", q=PH),
                        tp_sb[:, 64 + hg * 8:64 + (hg + 1) * 8]
                        .rearrange("p (h o) -> p h o", o=1)
                        .broadcast_to([128, 8, PH]), ALU.mult),
                        wait=[(s_te, te_y), (s_sc, sc_tp)], inc=s_ve)
                    psf[5] = [s_ve.at()]
                    pr.do("vector", lambda e, hg=hg: e.tensor_add(
                        ch_e[:, hg * 512:(hg + 1) * 512],
                        ch_e[:, hg * 512:(hg + 1) * 512], ps[4][:, 0:512]), inc=s_ve)
                    psf[4] = [s_ve.at()]
                v_y = s_ve.n
                pr.do("vector", lambda e: e.tensor_mul(ch_d[:], ch_d[:], drep_v[:]),
                      wait=[(s_sc, sc_drep)], inc=s_ve)
                pr.do("vector", lambda e: e.scalar_tensor_tensor(
                    out=ch_e[:], in0=ch_e[:], scalar=SSD_SCALE, in1=ch_d[:],
                    op0=ALU.mult, op1=ALU.add), inc=s_ve)
                v_yd = s_ve.n
                for hf in range(2):
                    sl = slice(hf * 2048, (hf + 1) * 2048)
                    pr.do("scalar", lambda e, sl=sl: e.activation(
                        out=ch_f[:, 0:2048], in_=ch_b[:, sl], func=AF.Exp, scale=-1.0),
                        wait=[s_ve.at(), (s_dw, w_z)], inc=s_sc)
                    pr.do("scalar", lambda e: e.activation(
                        out=ch_f[:, 0:2048], in_=ch_f[:, 0:2048], func=AF.Ln, bias=1.0),
                        inc=s_sc)
                    pr.do("scalar", lambda e: e.activation(
                        out=ch_f[:, 0:2048], in_=ch_f[:, 0:2048], func=AF.Exp,
                        scale=-1.0), inc=s_sc)
                    sc_sig = s_sc.n
                    pr.do("vector", lambda e, sl=sl: e.tensor_mul(
                        ch_e[:, sl], ch_e[:, sl], ch_b[:, sl]),
                        wait=[(s_sc, sc_sig)], inc=s_ve)
                    pr.do("vector", lambda e, sl=sl: e.tensor_mul(
                        ch_e[:, sl], ch_e[:, sl], ch_f[:, 0:2048]), inc=s_ve)
                for hf in range(2):
                    sl = slice(hf * 2048, (hf + 1) * 2048)
                    pr.do("vector", lambda e, sl=sl, hf=hf: e.scalar_tensor_tensor(
                        out=ch_f[:, 0:2048], in0=ch_e[:, sl], scalar=1.0,
                        in1=ch_e[:, sl], op0=ALU.mult, op1=ALU.mult,
                        accum_out=sums_sb[:, 9 + hf:10 + hf]), inc=s_ve)
                pr.do("vector", lambda e: e.tensor_add(
                    sums_sb[:, 11:12], sums_sb[:, 9:10], sums_sb[:, 10:11]),
                    wait=[s_ve.at()], inc=s_ve)
                v_ms = s_ve.n
                pr.do("scalar", lambda e: e.activation(
                    out=sums_sb[:, 12:13], in_=sums_sb[:, 11:12], func=AF.Ln,
                    scale=1.0 / DI, bias=eps2_sb[:]), wait=[(s_ve, v_ms)], inc=s_sc)
                pr.do("scalar", lambda e: e.activation(
                    out=sums_sb[:, 12:13], in_=sums_sb[:, 12:13], func=AF.Exp,
                    scale=-0.5), inc=s_sc)
                sc_rs = s_sc.n
                pr.do("vector", lambda e: e.scalar_tensor_tensor(
                    out=ch_a[:], in0=ch_e[:], scalar=sums_sb[:, 12:13], in1=nrmw_v[:],
                    op0=ALU.mult, op1=ALU.mult), wait=[(s_sc, sc_rs)], inc=s_ve)
                v_yn = s_ve.n
                for g in range(8):
                    b = g % 2
                    for jj in range(4):
                        j = g * 4 + jj
                        pr.do("tensor", lambda e, j=j, b=b, jj=jj: e.transpose(
                            pt[b][:, jj * 128:(jj + 1) * 128],
                            ch_a[:, j * 128:(j + 1) * 128], eye_sb[:]),
                            wait=[(s_ve, v_yn), (s_sc, pt_evac[b])], inc=s_te)
                    wt = s_te.n
                    half = (g % 2) * 512
                    pr.do("scalar", lambda e, b=b, hf2=half: e.copy(
                        out=stg_sb[:, hf2:hf2 + 512], in_=pt[b][:, 0:512]),
                        wait=[(s_te, wt), (s_out, stg_free[g % 2])], inc=s_sc)
                    pt_evac[b] = s_sc.n
                    wsc = s_sc.n
                    stg_free[g % 2] = dout(
                        ynt_sp[k, g * 4:(g + 1) * 4, :, :].rearrange("j p t -> p j t"),
                        stg_sb[:, half:half + 512].rearrange("p (j t) -> p j t", j=4),
                        wait=[(s_sc, wsc)])
            ynt_done = s_out.n
            for tth in range(2):
                for cgh in range(2):
                    for kk in range(KDI):
                        wl = dsync(wk_bf[:, 0:1024],
                                   cc_wo_o[kk * 128:(kk + 1) * 128,
                                           cgh * 1024:(cgh + 1) * 1024],
                                   wait=[(s_cc, cc_wo), s_te.at()])
                        w_yt = dsync(ch_a[:, 0:256].rearrange("p (c t) -> p c t", c=2),
                                     ynt_sp[sc * CPS + tth * 2:sc * CPS + tth * 2 + 2,
                                            kk, :, :].rearrange("c p t -> p c t"),
                                     wait=[(s_out, ynt_done), s_te.at()])
                        for q in range(4):
                            tt, cg = q // 2, q % 2
                            pr.do("tensor", lambda e, kk=kk, tt=tt, cg=cg, q=q:
                                  e.matmul(ps[q][:, 0:512],
                                           ch_a[:, tt * 128:(tt + 1) * 128],
                                           wk_bf[:, cg * 512:(cg + 1) * 512],
                                           start=(kk == 0), stop=(kk == KDI - 1)),
                                  wait=[(s_dw, max(wl, w_yt))] +
                                  (psf[q] if kk == 0 else []), inc=s_te)
                    te_wo = s_te.n
                    for tt in range(2):
                        ch = sc * CPS + tth * 2 + tt
                        wx = dsync(x_f[:], x_in[ch * 128:(ch + 1) * 128, :],
                                   wait=[s_ve.at()])
                        for cg in range(2):
                            q = tt * 2 + cg
                            col = cgh * 1024 + cg * 512
                            pr.do("vector", lambda e, q=q, col=col: e.tensor_add(
                                stg_sb[:, (q % 4) * 512:(q % 4 + 1) * 512],
                                x_f[:, col:col + 512], ps[q][:, 0:512]),
                                wait=[(s_te, te_wo), (s_dw, wx),
                                      (s_out, stg_free[q % 4])], inc=s_ve)
                            psf[q] = [s_ve.at()]
                            wv = s_ve.n
                            stg_free[q % 4] = dout(
                                x2_sp[ch * 128:(ch + 1) * 128, col:col + 512],
                                stg_sb[:, (q % 4) * 512:(q % 4 + 1) * 512],
                                wait=[(s_ve, wv)])
        b_done = s_out.n

    if debug and stage >= 5:
        d7 = mkdbg("dbg_x2", [256, D], BF16)
        dout(d7[:], x2_sp[0:256, :], wait=[(s_out, b_done)])


    if stage >= 6:
        moe_xln2 = wk_bf[:, 0:8192]
        moe_xln2b = ar_bf[:, 0:8192]
        for kk in range(KD):
            w_wg = dsync(etmp_sb[0:8, :], wg_in[:, kk * 128:(kk + 1) * 128],
                         wait=[s_te.at(), s_sc.at()])
            pr.do("tensor", lambda e, kk=kk: e.transpose(
                ps[4][:, 0:8], etmp_sb[0:8, :], eyef_sb[0:8, 0:8]),
                wait=[(s_dw, w_wg)] + psf[4], inc=s_te)
            wt = s_te.n
            pr.do("scalar", lambda e, kk=kk: e.copy(
                out=w3t_v[:, 128 + kk * 8:128 + (kk + 1) * 8], in_=ps[4][:, 0:8]),
                wait=[(s_te, wt)], inc=s_sc)
            psf[4] = [s_sc.at()]
        sc_wgt = s_sc.n
        xl2_prev = 0
        for sc in range(NSC):
            xd2 = emit_xln_superchunk(sc, m2_sb, rs2_sb, x2_sp, True,
                                      wait_x=[(s_out, b_done)], cast=True,
                                      fm_free=[(s_out, xl2_prev)])
            for kk in range(KD):
                dout(xln2_sp[kk * 128:(kk + 1) * 128, sc * SC:(sc + 1) * SC],
                     xln_fm[:, kk * SC:(kk + 1) * SC], wait=[(s_sc, xd2)])
            xl2_prev = s_out.n
            for tt in range(CPS):
                for kk in range(KD):
                    pr.do("tensor", lambda e, kk=kk, tt=tt: e.matmul(
                        ps[4][:, 0:8],
                        xln_fm[:, kk * SC + tt * 128:kk * SC + (tt + 1) * 128],
                        w3t_v[:, 128 + kk * 8:128 + (kk + 1) * 8],
                        start=(kk == 0), stop=(kk == KD - 1)),
                        wait=[(s_sc, max(xd2, sc_wgt))] + psf[4], inc=s_te)
                wt = s_te.n
                pr.do("scalar", lambda e, sc=sc, tt=tt: e.copy(
                    out=scs_sb[:, (sc * CPS + tt) * 8:(sc * CPS + tt + 1) * 8],
                    in_=ps[4][:, 0:8]), wait=[(s_te, wt)], inc=s_sc)
                psf[4] = [s_sc.at()]
        sc_scs = s_sc.n
        xl2_all = s_out.n
        for tile in range(NCH):
            sl = slice(tile * 8, (tile + 1) * 8)
            pr.do("vector", lambda e, sl=sl: e.max(
                out=p12_sb[:, 0:8], in_=scs_sb[:, sl]),
                wait=[(s_sc, sc_scs)], inc=s_ve)
            pr.do("vector", lambda e: e.tensor_sub(
                p12_sb[:, 8:9], p12_sb[:, 0:1], p12_sb[:, 1:2]),
                wait=[s_ve.at()], inc=s_ve)
            v_d12 = s_ve.n
            pr.do("scalar", lambda e: e.activation(
                out=p12_sb[:, 9:10], in_=p12_sb[:, 8:9], func=AF.Tanh, scale=0.5),
                wait=[(s_ve, v_d12)], inc=s_sc)
            sc_th = s_sc.n
            pr.do("vector", lambda e: e.tensor_scalar(
                out=p12_sb[:, 10:11], in0=p12_sb[:, 9:10], scalar1=0.5, scalar2=0.5,
                op0=ALU.mult, op1=ALU.add), wait=[(s_sc, sc_th)], inc=s_ve)
            pr.do("vector", lambda e: e.tensor_scalar(
                out=p12_sb[:, 11:12], in0=p12_sb[:, 9:10], scalar1=-0.5, scalar2=0.5,
                op0=ALU.mult, op1=ALU.add), inc=s_ve)
            pr.do("vector", lambda e, sl=sl: e.tensor_tensor(
                p12_sb[:, 16:24], scs_sb[:, sl],
                p12_sb[:, 0:1].broadcast_to([128, 8]), ALU.is_equal), inc=s_ve)
            pr.do("vector", lambda e, sl=sl: e.tensor_tensor(
                p12_sb[:, 24:32], scs_sb[:, sl],
                p12_sb[:, 1:2].broadcast_to([128, 8]), ALU.is_equal), inc=s_ve)
            pr.do("vector", lambda e, sl=sl: e.tensor_scalar_mul(
                wgt_sb[:, sl], p12_sb[:, 16:24], p12_sb[:, 10:11]), inc=s_ve)
            pr.do("vector", lambda e, sl=sl: e.scalar_tensor_tensor(
                out=wgt_sb[:, sl], in0=p12_sb[:, 24:32], scalar=p12_sb[:, 11:12],
                in1=wgt_sb[:, sl], op0=ALU.mult, op1=ALU.add), inc=s_ve)
            v_w = s_ve.n
            pr.do("tensor", lambda e, sl=sl: e.transpose(
                ps[4][:, 0:128][0:8, :] if False else ps[4][0:8, 0:128],
                wgt_sb[:, sl], eye_sb[:] if False else eyef_sb[:]),
                wait=[(s_ve, v_w)] + psf[4], inc=s_te)
            wt = s_te.n
            pr.do("scalar", lambda e, tile=tile: e.copy(
                out=wgtt_sb[:, tile * 128:(tile + 1) * 128], in_=ps[4][0:8, 0:128]),
                wait=[(s_te, wt)], inc=s_sc)
            psf[4] = [s_sc.at()]
        sc_w = s_sc.n
        g_wgsp = dgps(wg_sp[:], wgtt_sb[:], wait=[(s_sc, sc_w)])

        w1buf = [ch_a[:, 0:2048], ch_a[:, 2048:4096]]
        w2buf = [ch_c[:, 0:2048], ch_c[:, 2048:4096]]
        w3buf = [ch_b[:, 0:1408], ch_b[:, 1408:2816]]
        hq1_v = ar_bf[:, 13824:19456]
        init_free = [s_te.at(), s_sc.at(), s_ve.at()]
        w12_free = [list(init_free), list(init_free)]
        w3_free = [list(init_free), list(init_free)]
        g_ylast = [0, 0]
        stgf_free = [0, 0]
        gw12 = 0
        gw3 = 0
        wrow_te = 0
        for th in range(2):
            t0c = th * 1024
            w_xl = [0, 0]
            for q in range(2):
                buf = moe_xln2 if q == 0 else moe_xln2b
                w_xl[q] = dsync(
                    buf[:].rearrange("p (k c) -> p k c", k=KD),
                    xln2_sp[:, t0c + q * 512:t0c + (q + 1) * 512]
                    .rearrange("(k p) c -> p k c", p=128),
                    wait=[(s_out, xl2_all), s_te.at(), s_ve.at()])
            xl_w = w_xl[1]
            for e_i in range(E):
                for q in range(2):
                    w_wrow = dsync(
                        x_f[0:1, 0:512],
                        wg_sp[e_i:e_i + 1, t0c + q * 512:t0c + (q + 1) * 512],
                        wait=[(s_dg, g_wgsp)] +
                        ([(s_te, wrow_te)] if wrow_te else [s_te.at()]))
                    pr.do("tensor", lambda e: e.matmul(
                        ps[5][:, 0:512], ones1f[0:1, 0:128], x_f[0:1, 0:512],
                        start=True, stop=True),
                        wait=[(s_dw, w_wrow)] + psf[5], inc=s_te)
                    wrow_te = s_te.n
                    pr.do("scalar", lambda e, q=q: e.copy(
                        out=stg_sb[:, 1024 + q * 512:1024 + (q + 1) * 512],
                        in_=ps[5][:, 0:512]),
                        wait=[(s_te, wrow_te), s_ve.at()], inc=s_sc)
                    psf[5] = [s_sc.at()]
                sc_wt = s_sc.n
                for m in range(HT):
                    b = gw12 % 2
                    gw12 += 1
                    r0 = (e_i * HT + m) * 128
                    dsync(w1buf[b], cc_w1_o[r0:r0 + 128, :],
                          wait=[(s_cc, cc_w3)] + w12_free[b])
                    wl2 = dsync(w2buf[b], cc_w2_o[r0:r0 + 128, :])
                    for kk in range(KD):
                        for q in range(2):
                            xbuf = moe_xln2 if q == 0 else moe_xln2b
                            pr.do("tensor", lambda e, kk=kk, q=q, xb=xbuf, b=b:
                                  e.matmul(
                                      ps[q][:, 0:512],
                                      w1buf[b][:, kk * 128:(kk + 1) * 128],
                                      xb[:, kk * 512:(kk + 1) * 512],
                                      start=(kk == 0), stop=(kk == KD - 1)),
                                  wait=([(s_dw, wl2), (s_dw, xl_w)] + psf[q])
                                  if kk == 0 else [], inc=s_te)
                            pr.do("tensor", lambda e, kk=kk, q=q, xb=xbuf, b=b:
                                  e.matmul(
                                      ps[2 + q][:, 0:512],
                                      w2buf[b][:, kk * 128:(kk + 1) * 128],
                                      xb[:, kk * 512:(kk + 1) * 512],
                                      start=(kk == 0), stop=(kk == KD - 1)),
                                  wait=psf[2 + q] if kk == 0 else [], inc=s_te)
                    te_h = s_te.n
                    w12_free[b] = [(s_te, te_h)]
                    for q in range(2):
                        pr.do("scalar", lambda e, q=q: e.activation(
                            out=stg_sb[:, 0:512], in_=ps[q][:, 0:512], func=AF.Silu),
                            wait=[(s_te, te_h), s_ve.at(), (s_sc, sc_wt)], inc=s_sc)
                        sc_si = s_sc.n
                        hdst = (h_v if q == 0 else hq1_v)[:, m * 512:(m + 1) * 512]
                        pr.do("vector", lambda e, q=q, hd=hdst: e.tensor_tensor(
                            hd, stg_sb[:, 0:512], ps[2 + q][:, 0:512], ALU.mult),
                            wait=[(s_sc, sc_si)], inc=s_ve)
                        psf[q] = [s_ve.at()]
                        psf[2 + q] = [s_ve.at()]
                        pr.do("vector", lambda e, q=q, hd=hdst: e.tensor_tensor(
                            hd, hd,
                            stg_sb[:, 1024 + q * 512:1024 + (q + 1) * 512], ALU.mult),
                            inc=s_ve)
                v_h = s_ve.n
                for dk in range(KD):
                    b3 = gw3 % 2
                    gw3 += 1
                    r0 = (e_i * KD + dk) * 128
                    wl3 = dsync(w3buf[b3], cc_w3_o[r0:r0 + 128, :],
                                wait=[(s_cc, cc_w3)] + w3_free[b3])
                    for q in range(2):
                        hsv = h_v if q == 0 else hq1_v
                        for hk in range(HT):
                            pr.do("tensor", lambda e, hk=hk, q=q, b3=b3, hsv=hsv:
                                  e.matmul(
                                      ps[4 + q][:, 0:512],
                                      w3buf[b3][:, hk * 128:(hk + 1) * 128],
                                      hsv[:, hk * 512:(hk + 1) * 512],
                                      start=(hk == 0), stop=(hk == HT - 1)),
                                  wait=([(s_dw, wl3), (s_ve, v_h)] + psf[4 + q])
                                  if hk == 0 else [], inc=s_te)
                    te_y3 = s_te.n
                    w3_free[b3] = [(s_te, te_y3)]
                    for q in range(2):
                        pr.do("scalar", lambda e, q=q: e.copy(
                            out=stgf_sb[:, q * 512:(q + 1) * 512],
                            in_=ps[4 + q][:, 0:512]),
                            wait=[(s_te, te_y3), (s_dg, g_ylast[q])], inc=s_sc)
                        psf[4 + q] = [s_sc.at()]
                        wsc = s_sc.n
                        g_ylast[q] = dgps(
                            y_hbm[dk * 128:(dk + 1) * 128,
                                  t0c + q * 512:t0c + (q + 1) * 512],
                            stgf_sb[:, q * 512:(q + 1) * 512],
                            wait=[(s_sc, wsc)],
                            accum=(None if e_i == 0 else ALU.add))
        g_moe = s_dg.n
        for ch in range(NCH):
            wy = dsync(x_f[:].rearrange("p (k t) -> p k t", k=KD),
                       y_hbm[:, ch * 128:(ch + 1) * 128]
                       .rearrange("(k p) t -> p k t", p=128),
                       wait=[(s_dg, g_moe), s_ve.at(), s_te.at()])
            wx2 = dsync(ch_d[:, 0:2048], x2_sp[ch * 128:(ch + 1) * 128, :],
                        wait=[(s_out, b_done), s_ve.at()])
            for g4 in range(4):
                for j in range(4):
                    pr.do("tensor", lambda e, g4=g4, j=j: e.transpose(
                        ps[g4][:, j * 128:(j + 1) * 128],
                        x_f[:, (g4 * 4 + j) * 128:(g4 * 4 + j + 1) * 128],
                        eyef_sb[:]),
                        wait=[(s_dw, wy)] + psf[g4], inc=s_te)
                wt = s_te.n
                pr.do("vector", lambda e, g4=g4: e.tensor_add(
                    stgf_sb[:, (g4 % 2) * 512:(g4 % 2 + 1) * 512],
                    ch_d[:, g4 * 512:(g4 + 1) * 512], ps[g4][:, 0:512]),
                    wait=[(s_te, wt), (s_dw, wx2), (s_out, stgf_free[g4 % 2])],
                    inc=s_ve)
                psf[g4] = [s_ve.at()]
                wv = s_ve.n
                stgf_free[g4 % 2] = dout(
                    out_t[ch * 128:(ch + 1) * 128, g4 * 512:(g4 + 1) * 512],
                    stgf_sb[:, (g4 % 2) * 512:(g4 % 2 + 1) * 512],
                    wait=[(s_ve, wv)])
        moe_end = s_out.n
    pr.do("vector", lambda e: e.engine_nop(), wait=[(s_out, s_out.n), (s_dw, s_dw.n)])

    pr.replay(blk)
    st.close()
    return nc, dbg


def make_in_maps(x, W_in, W_out, dt_bias, A_log, D_param, B_bias, C_bias,
                 rope_freqs, norm_w, Wg, W1, W2, W3):
    x = np.asarray(x)
    W_in = np.ascontiguousarray(np.asarray(W_in, dtype=np.float32))
    W_out = np.ascontiguousarray(np.asarray(W_out, dtype=np.float32))
    W1 = np.asarray(W1, np.float32)
    W2 = np.asarray(W2, np.float32)
    W3 = np.asarray(W3, np.float32)
    maps = []
    for c in range(NCORES):
        b, h = c // 2, c % 2
        m = dict(
            x_in=np.ascontiguousarray(x[b, h * T:(h + 1) * T, :], dtype=np.float32),
            w_dtbc=np.ascontiguousarray(W_in[:, 8192:8384]),
            dt_bias=np.asarray(dt_bias, np.float32).reshape(NH, 1),
            a_log=np.asarray(A_log, np.float32).reshape(NH, 1),
            d_param=np.asarray(D_param, np.float32).reshape(1, NH),
            b_bias=np.asarray(B_bias, np.float32).reshape(1, NH * NS),
            c_bias=np.asarray(C_bias, np.float32).reshape(1, NH * NS),
            freqs_in=np.asarray(rope_freqs, np.float32).reshape(1, NH * 32),
            norm_in=np.asarray(norm_w, np.float32).reshape(1, DI),
            wg_in=np.ascontiguousarray(np.asarray(Wg, np.float32)),
            wzxs_share=np.ascontiguousarray(W_in[c * 256:(c + 1) * 256, 0:8192]),
            wout_share=np.ascontiguousarray(W_out[c * 512:(c + 1) * 512, :]),
            w1_mine=np.ascontiguousarray(W1[c]),
            w2_mine=np.ascontiguousarray(W2[c]),
            w3_mine=np.ascontiguousarray(W3[c]),
            mask_l=np.full((1, 1), float(h), dtype=np.float32),
        )
        maps.append(m)
    return maps


def kernel(L0_attn_x, L0_attn_W_in, L0_attn_W_out, L0_attn_dt_bias, L0_attn_A_log,
           L0_attn_D_param, L0_attn_B_bias, L0_attn_C_bias, L0_attn_rope_freqs,
           L0_attn_norm_w, L0_ffn_Wg, L0_ffn_W1, L0_ffn_W2, L0_ffn_W3):
    nc, _ = build(stage=99, debug=False)
    in_maps = make_in_maps(
        L0_attn_x, L0_attn_W_in, L0_attn_W_out, L0_attn_dt_bias, L0_attn_A_log,
        L0_attn_D_param, L0_attn_B_bias, L0_attn_C_bias, L0_attn_rope_freqs,
        L0_attn_norm_w, L0_ffn_Wg, L0_ffn_W1, L0_ffn_W2, L0_ffn_W3)
    res = run_bass_kernel_spmd(nc, in_maps, core_ids=list(range(NCORES)))
    out = np.empty((4, 4096, 2048), dtype=np.float32)
    for c in range(NCORES):
        b, h = c // 2, c % 2
        out[b, h * T:(h + 1) * T, :] = res.results[c]["out"]
    return out


if __name__ == "__main__":
    nc, _ = build(stage=2, debug=True)
    print("built ok")
